# version 12
# baseline (speedup 1.0000x reference)
import math
from contextlib import ExitStack

import numpy as np

import concourse.bass as bass
import concourse.mybir as mybir
from concourse.bass_utils import run_bass_kernel_spmd

F32 = mybir.dt.float32
BF16 = mybir.dt.bfloat16
I32 = mybir.dt.int32
AF = mybir.ActivationFunctionType
ALU = mybir.AluOpType

DM = 2048
DC = 16
T = 512
NT = 8
SPAN = 4608
H = 16
HK = 4
CAP = 20.0
NEG_MASK = -1.0e4
OSL = 256
NSL_BIG = 8
NSL_KV = 2

ENGINES = ("sync", "gpsimd", "tensor", "vector", "scalar")
SEM_OF = {"sync": "sQ", "gpsimd": "sG", "tensor": "sT", "vector": "sV", "scalar": "sS"}
DMA_SEMS = (["sC", "sCg", "sP", "sTB", "sCT", "sM", "sHB", "sH0", "sH1", "sO0", "sO1",
             "sW0", "sW1", "sW2"]
            + [f"sX{c}" for c in range(16)] + [f"sN{c}" for c in range(16)])



SQRT128_F32 = float(np.float32(np.sqrt(np.float32(128.0))))
S20 = 1.0 / (SQRT128_F32 * 20.0)
S20_HI = float(np.float32(S20))
S20_LO = float(np.float32(S20 - S20_HI))



class Sched:
    def __init__(self):
        self.ops = {e: [] for e in ENGINES}
        self.cnt = {s: 0 for s in list(SEM_OF.values()) + DMA_SEMS}
        self.lastwait = {}

    def emit(self, eng, fn, inc=0, sem=None):
        if sem is None:
            sem = SEM_OF[eng]
        if inc:
            self.cnt[sem] += inc
            self.ops[eng].append((fn, sem, inc))
            return (sem, self.cnt[sem])
        self.ops[eng].append((fn, None, 0))
        return None

    def wait(self, eng, *toks):
        own = SEM_OF[eng]
        for tok in toks:
            if tok is None:
                continue
            sem, val = tok
            key = (eng, sem)
            if self.lastwait.get(key, 0) >= val:
                continue
            self.lastwait[key] = val
            self.ops[eng].append((("W", sem, val), None, 0))

    def tensor_now(self):
        return ("sT", self.cnt["sT"])

    def vector_now(self):
        return ("sV", self.cnt["sV"])

    def scalar_now(self):
        return ("sS", self.cnt["sS"])


DBG = False
PAIR = True


def build_nc():
    nc = bass.Bass("TRN2", target_bir_lowering=False)

    x_h = nc.dram_tensor("x", [DM, SPAN], F32, kind="ExternalInput")
    wq_h = nc.dram_tensor("wq", [128, NSL_BIG, DC, OSL], F32, kind="ExternalInput")
    wk_h = nc.dram_tensor("wk", [128, NSL_KV, DC, OSL], F32, kind="ExternalInput")
    wv_h = nc.dram_tensor("wv", [128, NSL_KV, DC, OSL], F32, kind="ExternalInput")
    wo_h = nc.dram_tensor("wo", [128, NSL_BIG, DC, OSL], F32, kind="ExternalInput")
    wf_h = nc.dram_tensor("wf", [128, NSL_BIG, DC, OSL], F32, kind="ExternalInput")
    scale_h = nc.dram_tensor("scale", [128, DC], F32, kind="ExternalInput")
    bias_h = nc.dram_tensor("bias", [128, DC], F32, kind="ExternalInput")
    cos2_h = nc.dram_tensor("cos2_in", [128, SPAN], F32, kind="ExternalInput")
    sin2_h = nc.dram_tensor("sin2_in", [128, SPAN], F32, kind="ExternalInput")
    maskstr_h = nc.dram_tensor("mask_strips", [128, 256], F32, kind="ExternalInput")
    maskfst_h = nc.dram_tensor("mask_first", [128, 8 * T], F32, kind="ExternalInput")
    pmat_h = nc.dram_tensor("pmat", [128, 128], F32, kind="ExternalInput")
    ident_h = nc.dram_tensor("ident", [128, 128], F32, kind="ExternalInput")
    out_h = nc.dram_tensor("out", [DM, NT * T], F32, kind="ExternalOutput")
    dbg_h = {}
    if DBG:
        for nm, shp, dt in [("d_x1bf", [128, DC * T], BF16),
                            ("d_kbf0", [128, 2 * T], BF16),
                            ("d_vbfT", [128, 4 * T], BF16),
                            ("d_qbf0", [128, T], BF16),
                            ("d_e0", [128, 8 * T], BF16),
                            ("d_att", [128, DC * T], BF16),
                            ("d_x2bf", [128, DC * T], BF16),
                            ("d_rows", [1, 6144], BF16),
                            ("d_C2p", [128, T], F32),
                            ("d_S2p", [128, T], F32),
                            ("d_rr", [128, T], F32),
                            ("d_r2T", [128, 4], F32),
                            ("d_cos", [128, SPAN], F32),
                            ("d_sin", [128, SPAN], F32)]:
            dbg_h[nm] = nc.dram_tensor(nm, shp, dt, kind="ExternalOutput")

    WH = {"wq": wq_h, "wk": wk_h, "wv": wv_h, "wo": wo_h, "wf": wf_h}

    with ExitStack() as ctx:
        e = ctx.enter_context

        x_f = e(nc.sbuf_tensor("x_f", [128, DC * T], F32))
        xs_bf = e(nc.sbuf_tensor("xs_bf", [128, DC * T], BF16))
        xn_bf = e(nc.sbuf_tensor("xn_bf", [128, DC * T], BF16))
        att_bf = e(nc.sbuf_tensor("att_bf", [128, DC * T], BF16))
        e_bf = [e(nc.sbuf_tensor(f"e_bf{p}", [128, 8 * T], BF16)) for p in range(2)]
        kbf = [e(nc.sbuf_tensor(f"kbf{g}", [128, 2 * T], BF16)) for g in range(HK)]
        vbfT = [e(nc.sbuf_tensor(f"vbfT{p}", [128, 4 * T], BF16)) for p in range(2)]
        qbf = [e(nc.sbuf_tensor(f"qbf{p}", [128, T], BF16)) for p in range(2)]
        qpl = [e(nc.sbuf_tensor("qpl0", [128, T], BF16))]
        maskb = e(nc.sbuf_tensor("maskb", [128, 8 * T], BF16))
        maskstr = e(nc.sbuf_tensor("maskstr", [128, 256], BF16))
        wring = [e(nc.sbuf_tensor(f"wring{n}", [128, DC * OSL], BF16)) for n in range(3)]
        t1 = [e(nc.sbuf_tensor(f"t1_{p}", [128, T], F32)) for p in range(3)]
        tmp = [e(nc.sbuf_tensor(f"tmp{n}", [128, T], F32)) for n in range(6)]
        stage = [e(nc.sbuf_tensor(f"stage{p}", [128, T], F32)) for p in range(2)]
        rr_a = e(nc.sbuf_tensor("rr_a", [128, T], F32))
        rr_b = e(nc.sbuf_tensor("rr_b", [128, T], F32))
        rr_att = e(nc.sbuf_tensor("rr_att", [128, T], F32))
        recipS = [e(nc.sbuf_tensor(f"recipS{p}", [128, T], F32)) for p in range(2)]
        c2t = e(nc.sbuf_tensor("c2t", [128, T], F32))
        s2t = e(nc.sbuf_tensor("s2t", [128, T], F32))
        C2p = e(nc.sbuf_tensor("C2p", [128, T], F32))
        S2p = e(nc.sbuf_tensor("S2p", [128, T], F32))
        r2T = e(nc.sbuf_tensor("r2T", [128, 8], F32))
        xsq = [e(nc.sbuf_tensor(f"xsq{p}", [128, T], BF16)) for p in range(2)]
        rowf = e(nc.sbuf_tensor("rowf", [1, 2 * T], F32))
        rowbf = e(nc.sbuf_tensor("rowbf", [1, 6144], BF16))
        C_WQ, C_WF, C_WK, C_WV = 0, 2048, 4096, 4608
        NMU_A, NMU_B = 5120, 5632
        NMU_2 = NMU_A
        CROW = {"wq": C_WQ, "wf": C_WF, "wk": C_WK, "wv": C_WV}
        ones_bf = e(nc.sbuf_tensor("ones_bf", [128, 1], BF16))
        ones_fr = e(nc.sbuf_tensor("ones_fr", [1, 128], F32))
        half_fr = e(nc.sbuf_tensor("half_fr", [1, 128], F32))
        pmat_bf = e(nc.sbuf_tensor("pmat_bf", [128, 128], BF16))
        ident_f = e(nc.sbuf_tensor("ident_f", [128, 128], F32))
        biash = e(nc.sbuf_tensor("biash", [128, DC], F32))
        scale_sb = e(nc.sbuf_tensor("scale_sb", [128, DC], F32))
        bias_m20 = e(nc.sbuf_tensor("bias_m20", [128, 1], F32))
        sc20col = e(nc.sbuf_tensor("sc20col", [128, 1], F32))
        epscol = e(nc.sbuf_tensor("epscol", [1, 1], F32))

        pb = [e(nc.psum_tensor(f"pb{n}", [128, T], F32)) for n in range(8)]
        B_Q0, B_Q1, B_K0, B_K1, B_S, B_PV0, B_SW, B_PV1 = range(8)

        sems = {s: e(nc.semaphore(s)) for s in list(SEM_OF.values()) + DMA_SEMS}

        S = Sched()

        def dma(queue, out_ap, in_ap, *waits, sem):
            S.wait(queue, *waits)
            return S.emit(queue, lambda e, o=out_ap, i=in_ap: e.dma_start(out=o, in_=i),
                          inc=16, sem=sem)

        bank_free = [None] * 8
        bank_done = [None] * 8

        def mm_group(bank, parts):
            n = len(parts)
            S.wait("tensor", bank_free[bank])
            tok = None
            for j, (out_ap, lh, rh, waits) in enumerate(parts):
                S.wait("tensor", *waits)
                first, last = j == 0, j == n - 1

                def f(e, o=out_ap, a=lh, b=rh, st=first, sp=last):
                    return e.matmul(o, a, b, start=st, stop=sp, skip_group_check=True)
                tok = S.emit("tensor", f, inc=1 if last else 0)
            bank_done[bank] = tok
            return tok

        def vop(fn, *waits):
            S.wait("vector", *waits)
            return S.emit("vector", fn, inc=1)

        def sop(fn, *waits):
            S.wait("scalar", *waits)
            return S.emit("scalar", fn, inc=1)

        ring = {"next": 0, "free": [None, None, None]}

        def load_wslice(wname, sl):
            slot = ring["next"]
            ring["next"] = (slot + 1) % 3
            tok = dma("gpsimd", wring[slot][:, :], WH[wname].ap()[:, sl],
                      ring["free"][slot], sem=f"sW{slot}")
            return slot, tok

        def wtile(slot, c, j0, width=128):
            return wring[slot][:, c * OSL + j0: c * OSL + j0 + width]

        dbg_prev = [None]

        def dbg_dump(nm, sb_ap, tok):
            if DBG and nm in dbg_h:
                t = dma("sync", dbg_h[nm].ap(), sb_ap, tok, S.tensor_now(),
                        S.vector_now(), S.scalar_now(), dbg_prev[0], sem="sC")
                dbg_prev[0] = t
                S.wait("vector", t)
                S.wait("scalar", t)
                S.wait("tensor", t)
                S.wait("gpsimd", t)

        cvec_tok = {"wf": None, "wq": None, "wk": None, "wv": None}

        def emit_cvec(slot, wtok, wname, s):
            parts = [(pb[B_PV1][0:1, 0:OSL], ones_bf[:, :],
                      wring[slot][:, cc * OSL:(cc + 1) * OSL], [wtok, tok_ones_bf])
                     for cc in range(DC)]
            g = mm_group(B_PV1, parts)
            off = CROW[wname] + s * OSL
            tok = vop(lambda e, o=off: e.tensor_copy(rowbf[0:1, o:o + OSL],
                                                     pb[B_PV1][0:1, 0:OSL]), g)
            bank_free[B_PV1] = tok
            cvec_tok[wname] = tok

        dma("sync", scale_sb[:, :], scale_h.ap(), sem="sC")
        dma("sync", biash[:, :], bias_h.ap(), sem="sC")
        dma("sync", ident_f[:, :], ident_h.ap(), sem="sC")
        dma("gpsimd", pmat_bf[:, :], pmat_h.ap(), sem="sCg")
        tok_const = ("sC", S.cnt["sC"])
        tok_scale = tok_bias = tok_ident = tok_const
        tok_pmat = ("sCg", S.cnt["sCg"])
        dma("gpsimd", maskb[:, :], maskfst_h.ap(), sem="sM")
        dma("gpsimd", maskstr[:, :], maskstr_h.ap(), sem="sM")
        tok_mstr = ("sM", S.cnt["sM"])
        mask_cur = [tok_mstr]

        tok_ones_bf = vop(lambda e: e.memset(ones_bf[:, :], 1.0))
        tok_ones_fr = vop(lambda e: e.memset(ones_fr[:, :], 1.0))
        tok_half_fr = vop(lambda e: e.memset(half_fr[:, :], 0.5))
        tok_m20 = vop(lambda e: e.memset(bias_m20[:, :], -20.0))
        tok_eps = vop(lambda e: e.memset(epscol[:, :], 1e-5))
        tok_ms = vop(lambda e: e.memset(sc20col[:, :], S20_HI))
        tok_biash = vop(lambda e: e.tensor_single_scalar(biash[:, :], biash[:, :],
                                                         0.5, ALU.mult), tok_bias)
        tok_s20c = vop(lambda e: e.tensor_single_scalar(sc20col[:, :], sc20col[:, :],
                                                        S20_LO, ALU.add), tok_ms)

        tmp_free = [None, None]
        tmp2a_free = [None, None]
        tmp2b_free = [None, None]

        row_gate = [None]
        xsq_free = [None, None]

        def emit_ln_sums(src_bf, c, bx, bx2, first, last, waits=()):
            sq = xsq[c % 2]
            tok_sq = sop(lambda e, s=sq, i=src_bf[:, c * T:(c + 1) * T]:
                         e.activation(s[:, :], i, AF.Square),
                         *(list(waits) + [xsq_free[c % 2]]))
            S.wait("tensor", *waits)
            if first:
                S.wait("tensor", bank_free[bx], bank_free[bx2])

            S.wait("tensor", tok_ones_bf)

            def f1(e, i=src_bf[:, c * T:(c + 1) * T], st=first, sp=last, b=bx):
                return e.matmul(pb[b][0:1, :], ones_bf[:, :], i, start=st, stop=sp,
                                skip_group_check=True)
            t1_ = S.emit("tensor", f1, inc=1)
            S.wait("tensor", tok_sq)

            def f2(e, s=sq, st=first, sp=last, b=bx2):
                return e.matmul(pb[b][0:1, :], ones_bf[:, :], s[:, :], start=st, stop=sp,
                                skip_group_check=True)
            t2_ = S.emit("tensor", f2, inc=1)
            xsq_free[c % 2] = t2_
            if last:
                bank_done[bx] = ("sT", S.cnt["sT"])
                bank_done[bx2] = ("sT", S.cnt["sT"])

        def emit_ln_close(bx, bx2, nmu_off, rr_dst, *, half, repl_bank):
            M0, V0, R0 = rowf[0:1, 0:T], rowf[0:1, T:2 * T], rowf[0:1, 0:T]
            tsum, tsq = bank_done[bx], bank_done[bx2]
            tok_nmu = vop(lambda e: e.tensor_single_scalar(
                rowbf[0:1, nmu_off:nmu_off + T], pb[bx][0:1, :], -1.0 / DM, ALU.mult), tsum)
            tok_mu = vop(lambda e: e.tensor_single_scalar(M0, pb[bx][0:1, :],
                                                          1.0 / DM, ALU.mult),
                         tsum, row_gate[0])
            tok_m2 = vop(lambda e: e.tensor_mul(M0, M0, M0), tok_mu)
            tok_var = vop(lambda e: e.scalar_tensor_tensor(V0, pb[bx2][0:1, :], 1.0 / DM,
                                                           M0, ALU.mult, ALU.subtract),
                          tsq, tok_m2, row_gate[0])
            bank_free[bx] = tok_mu
            bank_free[bx2] = tok_var
            tok_ln = sop(lambda e: e.activation(V0, V0, AF.Ln, bias=epscol[:, :]),
                         tok_var, tok_eps)
            tok_r = sop(lambda e: e.activation(R0, V0, AF.Exp, scale=-0.5), tok_ln)
            lrow = half_fr if half else ones_fr
            ltok = tok_half_fr if half else tok_ones_fr
            tok_repl = mm_group(repl_bank, [(pb[repl_bank][:, :], lrow[0:1, :],
                                             R0, [tok_r, ltok])])
            tok_rr = vop(lambda e, d=rr_dst: e.tensor_copy(d[:, :], pb[repl_bank][:, :]),
                         tok_repl)
            bank_free[repl_bank] = tok_rr
            row_gate[0] = tok_repl
            return tok_nmu, tok_rr

        stage_free = [None, None]
        xh_consumed = [None, None]
        xh_ready = [None, None]
        x_ready = [None] * DC
        x2read = [None] * DC
        x1_done = [None] * DC

        def emit_ffn_pair(prev_spec, next_spec, first_weights):
            deferred = []

            def do_pass3(spec, c, tmp2t, tok_p2):
                mode = spec["mode"]
                if mode == "stage_out":
                    st = stage[c % 2]
                    tok3 = vop(lambda e, s=st, t2=tmp2t, cc=c:
                               e.tensor_add(s[:, :], t2[:, :], x_f[:, cc * T:(cc + 1) * T]),
                               tok_p2, stage_free[c % 2], x_ready[c])
                    x2read[c] = tok3
                    od = dma("sync", out_h.ap()[c * 128:(c + 1) * 128,
                                                spec["tile"] * T:(spec["tile"] + 1) * T],
                             st[:, :], tok3, sem=f"sO{c % 2}")
                    stage_free[c % 2] = od
                    pc = spec.get("post_chunk")
                    if pc is not None:
                        pc(c)
                    return tok3
                elif mode == "x_inplace":
                    tok3 = vop(lambda e, t2=tmp2t, cc=c:
                               e.tensor_add(x_f[:, cc * T:(cc + 1) * T], t2[:, :],
                                            x_f[:, cc * T:(cc + 1) * T]),
                               tok_p2, x_ready[c])
                    x_ready[c] = tok3
                    x1_done[c] = tok3
                    return tok3
                else:
                    st = stage[c % 2]
                    tok3 = vop(lambda e, s=st, t2=tmp2t, cc=c:
                               e.tensor_add(s[:, :], t2[:, :], t1[cc % 2][:, :]),
                               tok_p2, stage_free[c % 2], xh_ready[c % 2])
                    xh_consumed[c % 2] = tok3
                    tokc = vop(lambda e, s=st, cc=c:
                               e.tensor_copy(att_bf[:, cc * T:(cc + 1) * T], s[:, :]), tok3)
                    stage_free[c % 2] = tokc
                    x1_done[c] = tokc
                    return tokc

            for s in range(NSL_BIG):
                slot, wtok = load_wslice("wf", s)
                if first_weights:
                    emit_cvec(slot, wtok, "wf", s)
                last_use = None
                for member, spec in (("prev", prev_spec), ("next", next_spec)):
                    if spec is None:
                        continue
                    banks = (B_Q0, B_Q1) if member == "prev" else (B_K0, B_K1)
                    src = spec["src_bf"]
                    for jj in range(2):
                        c = s * 2 + jj
                        pc_pre = spec.get("pre_chunk")
                        if pc_pre is not None:
                            pc_pre(c)
                        bank = banks[c % 2]
                        parts = []
                        for cc in range(DC):
                            parts.append((pb[bank][:, :], wtile(slot, cc, jj * 128),
                                          src[:, cc * T:(cc + 1) * T],
                                          [wtok] + list(spec["src_toks"])))
                        parts.append((pb[bank][:, :],
                                      rowbf[0:1, C_WF + c * 128:C_WF + (c + 1) * 128],
                                      rowbf[0:1, spec["nmu_off"]:spec["nmu_off"] + T],
                                      [spec["nmu_tok"], cvec_tok["wf"]]))
                        g = mm_group(bank, parts)
                        last_use = g
                        p1t = tmp[c % 2]
                        tok1 = vop(lambda e, b=bank, t=p1t, r=spec["rr"]:
                                   e.tensor_mul(t[:, :], pb[b][:, :], r[:, :]),
                                   g, spec["rr_tok"], tmp_free[c % 2])
                        bank_free[bank] = tok1
                        if member == "prev":
                            t2t, tfree = tmp[2 + c % 2], tmp2a_free
                        else:
                            t2t, tfree = tmp[4 + c % 2], tmp2b_free
                            if len(deferred) >= 2:
                                cc0, tt, tk = deferred.pop(0)
                                tmp2b_free[cc0 % 2] = do_pass3(next_spec, cc0, tt, tk)
                        tok2 = sop(lambda e, t=p1t, t2=t2t, cc=c:
                                   e.activation(t2[:, :], t[:, :], AF.Identity,
                                                bias=biash[:, cc:cc + 1],
                                                scale=scale_sb[:, cc:cc + 1]),
                                   tok1, tok_biash, tok_scale, tfree[c % 2])
                        tmp_free[c % 2] = tok2
                        if member == "prev":
                            tmp2a_free[c % 2] = do_pass3(spec, c, t2t, tok2)
                        else:
                            deferred.append((c, t2t, tok2))
                ring["free"][slot] = last_use
            for cc0, tt, tk in deferred:
                tmp2b_free[cc0 % 2] = do_pass3(next_spec, cc0, tt, tk)

        qpl_free = [None]

        def emit_rope(bank, gtok, dst_bf, dst_gate, cos_tok, par=0):
            par = 0
            qp = qpl[par]
            tokc = vop(lambda e, b=bank, q=qp: e.tensor_copy(q[:, :], pb[b][:, :]),
                       gtok, qpl_free[par])
            sw = mm_group(B_SW, [(pb[B_SW][:, :], pmat_bf[:, :], qp[:, :],
                                  [tokc, tok_pmat])])
            u1, u2 = tmp[0], tmp[1]
            tok_u1 = vop(lambda e, b=bank: e.tensor_mul(u1[:, :], pb[b][:, :], C2p[:, :]),
                         gtok, cos_tok[0], tmp_free[0])
            tok_u2 = vop(lambda e: e.tensor_mul(u2[:, :], pb[B_SW][:, :], S2p[:, :]),
                         sw, cos_tok[1], tmp_free[1])
            tok_d = vop(lambda e, d=dst_bf: e.tensor_add(d, u1[:, :], u2[:, :]),
                        tok_u1, tok_u2, dst_gate)
            tmp_free[0] = tok_d
            tmp_free[1] = tok_d
            bank_free[bank] = tok_u1
            bank_free[B_SW] = tok_u2
            qpl_free[par] = sw
            return tok_d

        kv_dve = {"k": [None] * HK, "v": None}

        def emit_kv(tile, src_bf, src_toks, nmu_tok, rr_tok, cos_tok, r2T_tok,
                    first_weights, write_gate):
            half = tile % 2 if tile >= 0 else 1
            par = half
            for sl in range(NSL_KV):
                slot, wtok = load_wslice("wk", sl)
                if first_weights:
                    emit_cvec(slot, wtok, "wk", sl)
                last_g = None
                for gg in range(2):
                    g = sl * 2 + gg
                    bank = (B_Q0, B_Q1)[g % 2]
                    parts = []
                    for cc in range(DC):
                        parts.append((pb[bank][:, :], wtile(slot, cc, gg * 128),
                                      src_bf[:, cc * T:(cc + 1) * T],
                                      [wtok] + list(src_toks)))
                    parts.append((pb[bank][:, :],
                                  rowbf[0:1, C_WK + g * 128:C_WK + (g + 1) * 128],
                                  rowbf[0:1, NMU_2:NMU_2 + T],
                                  [nmu_tok, cvec_tok["wk"]]))
                    gt = mm_group(bank, parts)
                    last_g = gt
                    kv_dve["k"][g] = emit_rope(bank, gt, kbf[g][:, half * T:(half + 1) * T],
                                               write_gate, cos_tok)
                ring["free"][slot] = last_g
            for sl in range(NSL_KV):
                slot, wtok = load_wslice("wv", sl)
                if first_weights:
                    emit_cvec(slot, wtok, "wv", sl)
                last_g = None
                for tc in range(4):
                    bank = (B_K0, B_K1)[tc % 2]
                    parts = []
                    for cc in range(DC):
                        parts.append((pb[bank][:, 0:OSL],
                                      src_bf[:, cc * T + tc * 128: cc * T + (tc + 1) * 128],
                                      wring[slot][:, cc * OSL:(cc + 1) * OSL],
                                      [wtok] + list(src_toks)))
                    parts.append((pb[bank][:, 0:OSL],
                                  rowbf[0:1, NMU_2 + tc * 128:NMU_2 + (tc + 1) * 128],
                                  rowbf[0:1, C_WV + sl * OSL:C_WV + (sl + 1) * OSL],
                                  [nmu_tok, cvec_tok["wv"]]))
                    gt = mm_group(bank, parts)
                    last_g = gt
                    dst = vbfT[par][:, tc * T + sl * OSL: tc * T + (sl + 1) * OSL]
                    tokv = vop(lambda e, d=dst, b=bank, tcc=tc:
                               e.tensor_scalar(d, pb[b][:, 0:OSL], r2T[:, tcc:tcc + 1],
                                               None, ALU.mult),
                               gt, r2T_tok, write_gate)
                    bank_free[bank] = tokv
                    kv_dve["v"] = tokv
                ring["free"][slot] = last_g

        xn_free = [None]
        attn = {"e_free": [None, None], "attnout": [None] * H, "attnout_free": [None] * H}
        t1_free = [None, None, None]
        recipS_free = [None, None]
        kv_gate = [None]

        def emit_attention(tile, src_bf, src_toks, nmu_tok, rr_tok, cos_tok,
                           mask_tok, kv_all, first_weights):
            par = tile % 2
            kprev, kcur = 1 - par, par
            q_rope = [None] * (H + 1)
            qslot = [None]
            qbf_gate = [None, None]

            def group_closures(bank, parts, out):
                n = len(parts)
                cls = []
                for j, (out_ap, lh, rh, waits) in enumerate(parts):
                    def emit_j(j=j, out_ap=out_ap, lh=lh, rh=rh, waits=waits):
                        if j == 0:
                            S.wait("tensor", bank_free[bank])
                        S.wait("tensor", *waits)

                        def f(e, o=out_ap, a=lh, b=rh, st=(j == 0), sp=(j == n - 1)):
                            return e.matmul(o, a, b, start=st, stop=sp,
                                            skip_group_check=True)
                        tok = S.emit("tensor", f, inc=1 if j == n - 1 else 0)
                        if j == n - 1:
                            bank_done[bank] = tok
                            out[0] = tok
                    cls.append(emit_j)
                return cls

            def q_parts(h):
                if h >= H:
                    return None
                if h % 2 == 0:
                    qslot[0] = load_wslice("wq", h // 2)
                    if first_weights:
                        emit_cvec(qslot[0][0], qslot[0][1], "wq", h // 2)
                slot, wtok = qslot[0]
                bank = (B_Q0, B_Q1)[h % 2]
                parts = []
                for cc in range(DC):
                    parts.append((pb[bank][:, :], wtile(slot, cc, (h % 2) * 128),
                                  src_bf[:, cc * T:(cc + 1) * T],
                                  [wtok] + list(src_toks)))
                parts.append((pb[bank][:, :],
                              rowbf[0:1, C_WQ + h * 128:C_WQ + (h + 1) * 128],
                              rowbf[0:1, NMU_2:NMU_2 + T], [nmu_tok, cvec_tok["wq"]]))
                out = [None]
                return (h, bank, slot, out, group_closures(bank, parts, out))

            def q_finish(spec):
                if spec is None:
                    return
                h, bank, slot, out, _ = spec
                if h % 2 == 1:
                    ring["free"][slot] = out[0]
                q_rope[h] = emit_rope(bank, out[0], qbf[h % 2][:, :],
                                      qbf_gate[h % 2], cos_tok, par=h % 2)

            pv_state = {}

            def pv_a_parts(h):
                g = h // 4
                ep = e_bf[h % 2]
                pvb = (B_PV0, B_PV1)[h % 2]
                parts_pv, parts_s = [], []
                for kc in (3, 4, 0, 1, 2, 5, 6, 7):
                    khalf = kprev if kc < 4 else kcur
                    tc = kc % 4
                    tok_e, q0, q1 = e_toks_h[h][kc]
                    vt = vbfT[khalf][:, tc * T + g * 128: tc * T + (g + 1) * 128]
                    parts_pv.append((pb[pvb][:, q0:q1], vt,
                                     ep[:, kc * T + q0:kc * T + q1],
                                     [tok_e, kv_all, kv_dve["v"]]))
                    parts_s.append((pb[B_S][0:1, q0:q1], ones_bf[:, :],
                                    ep[:, kc * T + q0:kc * T + q1],
                                    [tok_e, tok_ones_bf]))
                out_pv, out_s = [None], [None]
                cls = group_closures(pvb, parts_pv, out_pv) +                     group_closures(B_S, parts_s, out_s)
                return (h, out_pv, out_s, cls)

            def pv_a_finish(spec):
                h, out_pv, out_s, _ = spec
                attn["e_free"][h % 2] = ("sT", S.cnt["sT"])
                tok_srow = sop(lambda e: e.copy(rowf[0:1, 0:T], pb[B_S][0:1, :]),
                               out_s[0], row_gate[0])
                bank_free[B_S] = tok_srow
                pv_state[h] = (out_pv[0], tok_srow)

            def pv_head_b(h):
                pv_g, tok_srow = pv_state.pop(h)
                pvb = (B_PV0, B_PV1)[h % 2]
                rep_g = mm_group(B_S, [(pb[B_S][:, :], ones_fr[0:1, :],
                                        rowf[0:1, 0:T], [tok_srow, tok_ones_fr])])
                row_gate[0] = rep_g
                tok_rec = vop(lambda e, p=h % 2: e.reciprocal(recipS[p][:, :],
                                                              pb[B_S][:, :]),
                              rep_g, recipS_free[h % 2])
                bank_free[B_S] = tok_rec
                tok_ao = vop(lambda e, b=pvb, p=h % 2, hh=h:
                             e.tensor_mul(att_bf[:, hh * T:(hh + 1) * T], pb[b][:, :],
                                          recipS[p][:, :]),
                             pv_g, tok_rec, attn["attnout_free"][h])
                recipS_free[h % 2] = tok_ao
                bank_free[pvb] = tok_ao
                attn["attnout"][h] = tok_ao

            e_toks_h = [None] * H
            fast = (tile > 0)
            qs0 = q_parts(0)
            for c in qs0[4]:
                c()
            q_finish(qs0)
            for h in range(H):
                g = h // 4
                qs = q_parts(h + 1)
                fill = list(qs[4]) if qs is not None else []
                pa = None
                if h > 0:
                    pa = pv_a_parts(h - 1)
                    fill += pa[3]
                e_toks = []
                for kc in range(8):
                    bank = (B_K0, B_K1)[kc % 2]
                    tsl = kc % 3
                    khalf = kprev if kc < 4 else kcur
                    if fast:
                        if kc < 4:
                            q0, q1 = 0, 128 * (kc + 1)
                            s0, moff = 128 * kc, 0
                        else:
                            q0, q1 = 128 * (kc - 4), T
                            s0, moff = 128 * (kc - 4), 128
                    else:
                        q0, q1, s0, moff = 0, T, None, None
                    lh = kbf[g][:, khalf * T + (kc % 4) * 128: khalf * T + (kc % 4) * 128 + 128]
                    sc_g = mm_group(bank, [(pb[bank][:, q0:q1], lh, qbf[h % 2][:, q0:q1],
                                            [q_rope[h], kv_all, kv_dve["k"][g]])])
                    tt = t1[tsl]
                    tok_t = sop(lambda e, b=bank, t=tt, a=q0, z=q1:
                                e.activation(t[:, a:z], pb[b][:, a:z], AF.Tanh,
                                             scale=sc20col[:, :]),
                                sc_g, tok_s20c, t1_free[tsl])
                    bank_free[bank] = tok_t
                    if fast:
                        tok_m = vop(lambda e, t=tt, a=s0, mo=moff:
                                    e.tensor_add(t[:, a:a + 128], t[:, a:a + 128],
                                                 maskstr[:, mo:mo + 128]),
                                    tok_t, tok_mstr)
                        tok_e = sop(lambda e, t=tt, kcc=kc, p=h % 2, a=q0, z=q1:
                                    e.activation(e_bf[p][:, kcc * T + a:kcc * T + z],
                                                 t[:, a:z], AF.Exp, scale=CAP,
                                                 bias=bias_m20[:, :]),
                                    tok_m, attn["e_free"][h % 2], tok_m20)
                    else:
                        tok_m = vop(lambda e, t=tt, kcc=kc:
                                    e.scalar_tensor_tensor(t[:, :], t[:, :], CAP,
                                                           maskb[:, kcc * T:(kcc + 1) * T],
                                                           ALU.mult, ALU.add),
                                    tok_t, mask_tok)
                        tok_e = sop(lambda e, t=tt, kcc=kc, p=h % 2:
                                    e.activation(e_bf[p][:, kcc * T:(kcc + 1) * T], t[:, :],
                                                 AF.Exp, bias=bias_m20[:, :]),
                                    tok_m, attn["e_free"][h % 2], tok_m20)
                    t1_free[tsl] = tok_e
                    e_toks.append((tok_e, q0, q1))
                    for _ in range(4):
                        if fill:
                            fill.pop(0)()
                for c in fill:
                    c()
                e_toks_h[h] = e_toks
                qbf_gate[h % 2] = ("sT", S.cnt["sT"])
                q_finish(qs)
                if pa is not None:
                    pv_a_finish(pa)
                if h > 0:
                    pv_head_b(h - 1)
            pa = pv_a_parts(H - 1)
            for c in pa[3]:
                c()
            pv_a_finish(pa)
            pv_head_b(H - 1)

        xnbf_tok = [None] * DC
        x2bf_tok = [None] * DC
        x1bf_tok = [None] * DC

        def emit_wo(tile, have_next, xsbf_gate):
            woslot = [None]
            for oc in range(DC):
                if oc % 2 == 0:
                    woslot[0] = load_wslice("wo", oc // 2)
                slot, wtok = woslot[0]
                bank = (B_Q0, B_Q1)[oc % 2]
                parts = []
                for cc in range(DC):
                    parts.append((pb[bank][:, :], wtile(slot, cc, (oc % 2) * 128),
                                  att_bf[:, cc * T:(cc + 1) * T],
                                  [wtok, attn["attnout"][cc]]))
                g = mm_group(bank, parts)
                if oc % 2 == 1:
                    ring["free"][slot] = g
                tok2 = vop(lambda e, b=bank, c=oc:
                           e.tensor_add(x_f[:, c * T:(c + 1) * T], pb[b][:, :],
                                        x_f[:, c * T:(c + 1) * T]),
                           g, x_ready[oc])
                bank_free[bank] = tok2
                x_ready[oc] = tok2
                tokc = vop(lambda e, c=oc:
                           e.tensor_copy(xs_bf[:, c * T:(c + 1) * T],
                                         x_f[:, c * T:(c + 1) * T]),
                           tok2, xsbf_gate)
                x2bf_tok[oc] = tokc
                emit_ln_sums(xs_bf, oc, B_S, B_PV0, oc == 0, oc == DC - 1, waits=(tokc,))
                if have_next:
                    emit_ln_sums(xn_bf, oc, B_K0, B_K1, oc == 0, oc == DC - 1,
                                 waits=(xnbf_tok[oc],))

        def emit_ln2_close(cos_toks):
            nmu2, rr2 = emit_ln_close(B_S, B_PV0, NMU_2, rr_att, half=False,
                                      repl_bank=B_SW)
            tc1 = vop(lambda e: e.tensor_mul(C2p[:, :], c2t[:, :], rr_att[:, :]),
                      rr2, cos_toks[0])
            tc2 = vop(lambda e: e.tensor_mul(S2p[:, :], s2t[:, :], rr_att[:, :]),
                      rr2, cos_toks[1])
            last_t = None
            for tc in range(4):
                S.wait("tensor", bank_free[B_S], rr2, tok_ident)

                def ftr(e, tcc=tc):
                    return e.transpose(pb[B_S][:, 0:128],
                                       rr_att[:, tcc * 128:(tcc + 1) * 128], ident_f[:, :])
                gtr = S.emit("tensor", ftr, inc=1)
                last_t = vop(lambda e, tcc=tc:
                             e.tensor_copy(r2T[:, tcc:tcc + 1], pb[B_S][:, 0:1]), gtr)
                bank_free[B_S] = last_t
            return nmu2, rr2, (tc1, tc2), last_t

        for c in range(DC):
            x_ready[c] = dma("sync", x_f[:, c * T:(c + 1) * T],
                             x_h.ap()[c * 128:(c + 1) * 128, T:2 * T], sem=f"sX{c}")
        for c in range(DC):
            dma("gpsimd", xs_bf[:, c * T:(c + 1) * T],
                x_h.ap()[c * 128:(c + 1) * 128, 0:T], sem="sHB")
        tok_hb = ("sHB", S.cnt["sHB"])
        h_xbf_tok = [tok_hb] * DC
        for c in range(DC):
            xnbf_tok[c] = dma("gpsimd", xn_bf[:, c * T:(c + 1) * T],
                              x_h.ap()[c * 128:(c + 1) * 128, T:2 * T], sem=f"sN{c}")

        for c in range(DC):
            emit_ln_sums(xs_bf, c, B_S, B_PV0, c == 0, c == DC - 1,
                         waits=(h_xbf_tok[c],))
        nmu_h, rr_h = emit_ln_close(B_S, B_PV0, NMU_A, rr_a, half=True, repl_bank=B_SW)
        for c in range(DC):
            emit_ln_sums(xn_bf, c, B_S, B_PV0, c == 0, c == DC - 1,
                         waits=(xnbf_tok[c],))
        nmu_0, rr_0 = emit_ln_close(B_S, B_PV0, NMU_B, rr_b, half=True, repl_bank=B_SW)

        def halo_pre_chunk(c):
            xh_ready[c % 2] = dma("sync", t1[c % 2][:, :],
                                  x_h.ap()[c * 128:(c + 1) * 128, 0:T],
                                  xh_consumed[c % 2], sem=f"sH{c % 2}")

        halo_spec = dict(src_bf=xs_bf, src_toks=[h_xbf_tok[DC - 1]], nmu_off=NMU_A,
                         nmu_tok=nmu_h, rr=rr_a, rr_tok=rr_h, mode="halo", tile=-1,
                         pre_chunk=halo_pre_chunk)
        t0_spec = dict(src_bf=xn_bf, src_toks=[xnbf_tok[DC - 1]], nmu_off=NMU_B,
                       nmu_tok=nmu_0, rr=rr_b, rr_tok=rr_0, mode="x_inplace", tile=0)
        if PAIR:
            emit_ffn_pair(halo_spec, t0_spec, first_weights=True)
        else:
            emit_ffn_pair(halo_spec, None, first_weights=True)
            emit_ffn_pair(None, t0_spec, first_weights=False)
        pair_done = S.tensor_now()

        for c in range(DC):
            x1bf_tok[c] = vop(lambda e, cc=c:
                              e.tensor_copy(xs_bf[:, cc * T:(cc + 1) * T],
                                            x_f[:, cc * T:(cc + 1) * T]),
                              x1_done[c], pair_done)

        dbg_dump("d_x1bf", xs_bf[:, :], x1bf_tok[DC - 1])
        for c in range(DC):
            emit_ln_sums(att_bf, c, B_S, B_PV0, c == 0, c == DC - 1,
                         waits=(x1_done[c],))
        dma("sync", c2t[:, :], cos2_h.ap()[:, 0:T], sem="sCT")
        st_h = dma("sync", s2t[:, :], sin2_h.ap()[:, 0:T], sem="sCT")
        nmu2_h, rr2_h, cs_h, r2T_h = emit_ln2_close((st_h, st_h))
        cs_free = cs_h[1]
        emit_kv(-1, att_bf, [x1_done[DC - 1]], nmu2_h, rr2_h, cs_h, r2T_h,
                first_weights=True, write_gate=None)
        kv_all = S.tensor_now()
        for hh in range(H):
            attn["attnout_free"][hh] = kv_all
        kv_gate[0] = None

        for i in range(NT):
            have_next = i < NT - 1
            dma("sync", c2t[:, :], cos2_h.ap()[:, (i + 1) * T:(i + 2) * T], cs_free, sem="sCT")
            st_i = dma("sync", s2t[:, :], sin2_h.ap()[:, (i + 1) * T:(i + 2) * T], cs_free, sem="sCT")
            ct_i = st_i
            for c in range(DC):
                emit_ln_sums(xs_bf, c, B_S, B_PV0, c == 0, c == DC - 1,
                             waits=(x1bf_tok[c],))
            nmu2, rr2, cs_toks, r2T_tok = emit_ln2_close((ct_i, st_i))
            cs_free = cs_toks[1]

            if have_next:
                xn_gate = xn_free[0] if i > 0 else pair_done
                for c in range(DC):
                    xnbf_tok[c] = dma("gpsimd", xn_bf[:, c * T:(c + 1) * T],
                                      x_h.ap()[c * 128:(c + 1) * 128,
                                               (i + 2) * T:(i + 3) * T], xn_gate,
                                      sem=f"sN{c}")

            src_toks = [x1bf_tok[DC - 1]]
            emit_kv(i, xs_bf, src_toks, nmu2, rr2, cs_toks, r2T_tok,
                    first_weights=False, write_gate=kv_gate[0])
            kv_all = S.tensor_now()

            if i == 0:
                dbg_dump("d_kbf0", kbf[0][:, :], kv_all)
                dbg_dump("d_vbfT", vbfT[0][:, :], kv_all)
                dbg_dump("d_C2p", C2p[:, :], cs_toks[1])
                dbg_dump("d_S2p", S2p[:, :], cs_toks[1])
                dbg_dump("d_rr", rr_att[:, :], rr2)
                dbg_dump("d_r2T", r2T[:, 0:4], r2T_tok)
                dbg_dump("d_cos", cos2_h.ap(), None)
                dbg_dump("d_sin", sin2_h.ap(), None)
            emit_attention(i, xs_bf, src_toks, nmu2, rr2, cs_toks, mask_cur[0],
                           kv_all, first_weights=(i == 0))
            if i == 0:
                dbg_dump("d_att", att_bf[:, :], attn["attnout"][H - 1])
                dbg_dump("d_qbf0", qbf[0][:, :], attn["attnout"][H - 1])
                dbg_dump("d_e0", e_bf[1][:, :], attn["attnout"][H - 1])
            kv_gate[0] = S.tensor_now()
            attn_done = S.tensor_now()


            emit_wo(i, have_next, xsbf_gate=attn_done)
            wo_done = S.tensor_now()
            if i == 0:
                dbg_dump("d_x2bf", xs_bf[:, :], x2bf_tok[DC - 1])
            for hh in range(H):
                attn["attnout_free"][hh] = wo_done

            nmu3, rr3 = emit_ln_close(B_S, B_PV0, NMU_A, rr_a, half=True, repl_bank=B_SW)
            if have_next:
                nmu1n, rr1n = emit_ln_close(B_K0, B_K1, NMU_B, rr_b, half=True,
                                            repl_bank=B_SW)

            def mk_post_chunk(ii):
                def post_chunk(c):
                    x_ready[c] = dma("sync", x_f[:, c * T:(c + 1) * T],
                                     x_h.ap()[c * 128:(c + 1) * 128,
                                              (ii + 2) * T:(ii + 3) * T], x2read[c],
                                     sem=f"sX{c}")
                return post_chunk

            prev_spec = dict(src_bf=xs_bf, src_toks=[x2bf_tok[DC - 1]], nmu_off=NMU_A,
                             nmu_tok=nmu3, rr=rr_a, rr_tok=rr3, mode="stage_out",
                             tile=i, post_chunk=mk_post_chunk(i) if have_next else None)
            next_spec = None
            if have_next:
                next_spec = dict(src_bf=xn_bf, src_toks=[xnbf_tok[DC - 1]],
                                 nmu_off=NMU_B, nmu_tok=nmu1n, rr=rr_b, rr_tok=rr1n,
                                 mode="x_inplace", tile=i + 1)
            if PAIR:
                emit_ffn_pair(prev_spec, next_spec, first_weights=False)
            else:
                emit_ffn_pair(prev_spec, None, first_weights=False)
                if next_spec is not None:
                    emit_ffn_pair(None, next_spec, first_weights=False)
            pair_done = S.tensor_now()
            xn_free[0] = pair_done
            if have_next:
                for c in range(DC):
                    x1bf_tok[c] = vop(lambda e, cc=c:
                                      e.tensor_copy(xs_bf[:, cc * T:(cc + 1) * T],
                                                    x_f[:, cc * T:(cc + 1) * T]),
                                      x1_done[c], pair_done)

        S.ops["sync"].append((("W", "sO0", S.cnt["sO0"]), None, 0))
        S.ops["sync"].append((("W", "sO1", S.cnt["sO1"]), None, 0))

        def realize_one(name):
            def f(engine):
                for fn, sem, inc in S.ops[name]:
                    if isinstance(fn, tuple) and fn[0] == "W":
                        engine.wait_ge(sems[fn[1]], fn[2])
                    else:
                        inst = fn(engine)
                        if sem is not None:
                            inst.then_inc(sems[sem], inc)
            return f

        block = e(nc.Block())
        block.sync(realize_one("sync"))
        block.gpsimd(realize_one("gpsimd"))
        block.tensor(realize_one("tensor"))
        block.vector(realize_one("vector"))
        block.scalar(realize_one("scalar"))

    return nc


def _layout_weight(w):
    O = w.shape[0]
    nsl = O // OSL
    wt = np.ascontiguousarray(w.T)
    L = wt.reshape(DC, 128, nsl, OSL).transpose(1, 2, 0, 3)
    return np.ascontiguousarray(L, dtype=np.float32)


def _build_masks(h):
    kk = np.arange(8 * 128)[:, None]
    a = np.arange(T)[None, :]
    allowed = (a <= kk) & (kk <= a + T)
    m = np.where(allowed, 0.0, NEG_MASK).astype(np.float32)
    m_std = np.ascontiguousarray(
        m.reshape(8, 128, T).transpose(1, 0, 2).reshape(128, 8 * T))
    if h == 0:
        mf2 = m.reshape(8, 128, T).copy()
        mf2[0:4] = NEG_MASK
        m_first = np.ascontiguousarray(mf2.transpose(1, 0, 2).reshape(128, 8 * T))
    else:
        m_first = m_std
    p = np.arange(128)[:, None]
    ap = np.arange(128)[None, :]
    strip_prev = np.where(ap <= p, 0.0, NEG_MASK / CAP).astype(np.float32)
    strip_cur = np.where(ap >= p, 0.0, NEG_MASK / CAP).astype(np.float32)
    m_strips = np.ascontiguousarray(np.concatenate([strip_prev, strip_cur], axis=1))
    return m_strips, m_first


def _rope_tables_f32():
    import jax
    import jax.numpy as jnp
    cpu = jax.devices("cpu")[0]
    with jax.default_device(cpu):
        inv = 1.0 / 10000.0 ** (jnp.arange(0, 128, 2, dtype=jnp.float32) / 128)
        ang = jnp.arange(8192, dtype=jnp.float32)[:, None] * inv
        cos = np.asarray(jnp.cos(ang))
        sin = np.asarray(jnp.sin(ang))
    c2 = np.repeat(cos.T, 2, axis=0)
    s2 = np.empty((128, 8192), dtype=np.float32)
    s2[0::2] = -sin.T
    s2[1::2] = sin.T
    return np.ascontiguousarray(c2), np.ascontiguousarray(s2)


def kernel(L0_attn_x, L0_attn_wq, L0_attn_wk, L0_attn_wv, L0_attn_wo,
           L0_ffn_weight, L0_ffn_scale_weight, L0_ffn_bias):
    x = np.asarray(L0_attn_x, dtype=np.float32)
    wq = _layout_weight(np.asarray(L0_attn_wq, dtype=np.float32))
    wk = _layout_weight(np.asarray(L0_attn_wk, dtype=np.float32))
    wv = _layout_weight(np.asarray(L0_attn_wv, dtype=np.float32))
    wo = _layout_weight(np.asarray(L0_attn_wo, dtype=np.float32))
    wf = _layout_weight(np.asarray(L0_ffn_weight, dtype=np.float32))
    scale = np.ascontiguousarray(
        np.asarray(L0_ffn_scale_weight, dtype=np.float32).reshape(DC, 128).T)
    bias = np.ascontiguousarray(
        np.asarray(L0_ffn_bias, dtype=np.float32).reshape(DC, 128).T)
    pmat = np.zeros((128, 128), dtype=np.float32)
    idx = np.arange(128)
    pmat[idx, idx ^ 1] = 1.0
    ident = np.eye(128, dtype=np.float32)
    c2full, s2full = _rope_tables_f32()

    nc = build_nc()
    in_maps = []
    for cid in range(8):
        b, h = cid // 2, cid % 2
        lo = h * 4096
        xc = np.zeros((SPAN, DM), dtype=np.float32)
        if h == 0:
            xc[T:] = x[b, 0:4096]
        else:
            xc[:] = x[b, 4096 - T:8192]
        xc = np.ascontiguousarray(xc.T)
        posv = np.maximum(np.arange(lo - T, lo + 4096), 0)
        cos2 = np.ascontiguousarray(c2full[:, posv])
        sin2 = np.ascontiguousarray(s2full[:, posv])
        m_strips, m_first = _build_masks(h)
        in_maps.append({
            "x": xc, "wq": wq, "wk": wk, "wv": wv, "wo": wo, "wf": wf,
            "scale": scale, "bias": bias, "cos2_in": cos2, "sin2_in": sin2,
            "mask_strips": m_strips, "mask_first": m_first,
            "pmat": pmat, "ident": ident,
        })
    res = run_bass_kernel_spmd(nc, in_maps, core_ids=list(range(8)))
    out = np.empty((4, 8192, DM), dtype=np.float32)
    for cid in range(8):
        b, h = cid // 2, cid % 2
        out[b, h * 4096:(h + 1) * 4096, :] = res.results[cid]["out"].T
    return out


# version 14 (final)
# speedup vs baseline: 1.2819x; 1.2819x over previous
import math
from contextlib import ExitStack

import numpy as np

import concourse.bass as bass
import concourse.mybir as mybir
from concourse.bass_utils import run_bass_kernel_spmd

F32 = mybir.dt.float32
BF16 = mybir.dt.bfloat16
I32 = mybir.dt.int32
AF = mybir.ActivationFunctionType
ALU = mybir.AluOpType

DM = 2048
DC = 16
T = 512
NT = 8
SPAN = 4608
H = 16
HK = 4
CAP = 20.0
NEG_MASK = -1.0e4
OSL = 256
NSL_BIG = 8
NSL_KV = 2

ENGINES = ("sync", "gpsimd", "tensor", "vector", "scalar")
SEM_OF = {"sync": "sQ", "gpsimd": "sG", "tensor": "sT", "vector": "sV", "scalar": "sS"}
DMA_SEMS = (["sC", "sCg", "sP", "sTB", "sCT", "sM", "sHB", "sH0", "sH1", "sO0", "sO1",
             "sW0", "sW1", "sW2"]
            + [f"sX{c}" for c in range(16)] + [f"sN{c}" for c in range(16)])



SQRT128_F32 = float(np.float32(np.sqrt(np.float32(128.0))))
S20 = 1.0 / (SQRT128_F32 * 20.0)
S20_HI = float(np.float32(S20))
S20_LO = float(np.float32(S20 - S20_HI))



class Sched:
    def __init__(self):
        self.ops = {e: [] for e in ENGINES}
        self.cnt = {s: 0 for s in list(SEM_OF.values()) + DMA_SEMS}
        self.lastwait = {}

    def emit(self, eng, fn, inc=0, sem=None):
        if sem is None:
            sem = SEM_OF[eng]
        if inc:
            self.cnt[sem] += inc
            self.ops[eng].append((fn, sem, inc))
            return (sem, self.cnt[sem])
        self.ops[eng].append((fn, None, 0))
        return None

    def wait(self, eng, *toks):
        own = SEM_OF[eng]
        for tok in toks:
            if tok is None:
                continue
            sem, val = tok
            key = (eng, sem)
            if self.lastwait.get(key, 0) >= val:
                continue
            self.lastwait[key] = val
            self.ops[eng].append((("W", sem, val), None, 0))

    def tensor_now(self):
        return ("sT", self.cnt["sT"])

    def vector_now(self):
        return ("sV", self.cnt["sV"])

    def scalar_now(self):
        return ("sS", self.cnt["sS"])


DBG = False
PAIR = True


def build_nc():
    nc = bass.Bass("TRN2", target_bir_lowering=False)

    x_h = nc.dram_tensor("x", [DM, SPAN], F32, kind="ExternalInput")
    wq_h = nc.dram_tensor("wq", [128, NSL_BIG, DC, OSL], F32, kind="ExternalInput")
    wk_h = nc.dram_tensor("wk", [128, NSL_KV, DC, OSL], F32, kind="ExternalInput")
    wv_h = nc.dram_tensor("wv", [128, NSL_KV, DC, OSL], F32, kind="ExternalInput")
    wo_h = nc.dram_tensor("wo", [128, NSL_BIG, DC, OSL], F32, kind="ExternalInput")
    wf_h = nc.dram_tensor("wf", [128, NSL_BIG, DC, OSL], F32, kind="ExternalInput")
    scale_h = nc.dram_tensor("scale", [128, DC], F32, kind="ExternalInput")
    bias_h = nc.dram_tensor("bias", [128, DC], F32, kind="ExternalInput")
    cos2_h = nc.dram_tensor("cos2_in", [128, SPAN], F32, kind="ExternalInput")
    sin2_h = nc.dram_tensor("sin2_in", [128, SPAN], F32, kind="ExternalInput")
    maskstr_h = nc.dram_tensor("mask_strips", [128, 256], F32, kind="ExternalInput")
    maskfst_h = nc.dram_tensor("mask_first", [128, 8 * T], F32, kind="ExternalInput")
    pmat_h = nc.dram_tensor("pmat", [128, 128], F32, kind="ExternalInput")
    ident_h = nc.dram_tensor("ident", [128, 128], F32, kind="ExternalInput")
    out_h = nc.dram_tensor("out", [DM, NT * T], F32, kind="ExternalOutput")
    dbg_h = {}
    if DBG:
        for nm, shp, dt in [("d_x1bf", [128, DC * T], BF16),
                            ("d_kbf0", [128, 2 * T], BF16),
                            ("d_vbfT", [128, 4 * T], BF16),
                            ("d_qbf0", [128, T], BF16),
                            ("d_e0", [128, 8 * T], BF16),
                            ("d_att", [128, DC * T], BF16),
                            ("d_x2bf", [128, DC * T], BF16),
                            ("d_rows", [1, 6144], BF16),
                            ("d_C2p", [128, T], F32),
                            ("d_S2p", [128, T], F32),
                            ("d_rr", [128, T], F32),
                            ("d_r2T", [128, 4], F32),
                            ("d_cos", [128, SPAN], F32),
                            ("d_sin", [128, SPAN], F32)]:
            dbg_h[nm] = nc.dram_tensor(nm, shp, dt, kind="ExternalOutput")

    WH = {"wq": wq_h, "wk": wk_h, "wv": wv_h, "wo": wo_h, "wf": wf_h}

    with ExitStack() as ctx:
        e = ctx.enter_context

        x_f = e(nc.sbuf_tensor("x_f", [128, DC * T], F32))
        xs_bf = e(nc.sbuf_tensor("xs_bf", [128, DC * T], BF16))
        xn_bf = e(nc.sbuf_tensor("xn_bf", [128, DC * T], BF16))
        att_bf = e(nc.sbuf_tensor("att_bf", [128, DC * T], BF16))
        e_bf = [e(nc.sbuf_tensor(f"e_bf{p}", [128, 8 * T], BF16)) for p in range(2)]
        kbf = [e(nc.sbuf_tensor(f"kbf{g}", [128, 2 * T], BF16)) for g in range(HK)]
        vbfT = [e(nc.sbuf_tensor(f"vbfT{p}", [128, 4 * T], BF16)) for p in range(2)]
        qbf = [e(nc.sbuf_tensor(f"qbf{p}", [128, T], BF16)) for p in range(2)]
        qpl = [e(nc.sbuf_tensor("qpl0", [128, T], BF16))]
        maskb = e(nc.sbuf_tensor("maskb", [128, 8 * T], BF16))
        maskstr = e(nc.sbuf_tensor("maskstr", [128, 256], BF16))
        wring = [e(nc.sbuf_tensor(f"wring{n}", [128, DC * OSL], BF16)) for n in range(3)]
        t1 = [e(nc.sbuf_tensor(f"t1_{p}", [128, T], F32)) for p in range(3)]
        tmp = [e(nc.sbuf_tensor(f"tmp{n}", [128, T], F32)) for n in range(6)]
        stage = [e(nc.sbuf_tensor(f"stage{p}", [128, T], F32)) for p in range(2)]
        rr_a = e(nc.sbuf_tensor("rr_a", [128, T], F32))
        rr_b = e(nc.sbuf_tensor("rr_b", [128, T], F32))
        rr_att = e(nc.sbuf_tensor("rr_att", [128, T], F32))
        mu_a = e(nc.sbuf_tensor("mu_a", [128, T], F32))
        mu_b = e(nc.sbuf_tensor("mu_b", [128, T], F32))
        mu_att = e(nc.sbuf_tensor("mu_att", [128, T], F32))
        recipS = [e(nc.sbuf_tensor(f"recipS{p}", [128, T], F32)) for p in range(2)]
        c2t = e(nc.sbuf_tensor("c2t", [128, T], F32))
        s2t = e(nc.sbuf_tensor("s2t", [128, T], F32))
        C2p = e(nc.sbuf_tensor("C2p", [128, T], F32))
        S2p = e(nc.sbuf_tensor("S2p", [128, T], F32))
        r2T = e(nc.sbuf_tensor("r2T", [128, 8], F32))
        xsq = [e(nc.sbuf_tensor(f"xsq{p}", [128, T], BF16)) for p in range(2)]
        rowf = e(nc.sbuf_tensor("rowf", [1, 2 * T], F32))
        ones_bf = e(nc.sbuf_tensor("ones_bf", [128, 1], BF16))
        ones_fr = e(nc.sbuf_tensor("ones_fr", [1, 128], F32))
        half_fr = e(nc.sbuf_tensor("half_fr", [1, 128], F32))
        pmat_bf = e(nc.sbuf_tensor("pmat_bf", [128, 128], BF16))
        ident_f = e(nc.sbuf_tensor("ident_f", [128, 128], F32))
        biash = e(nc.sbuf_tensor("biash", [128, DC], F32))
        scale_sb = e(nc.sbuf_tensor("scale_sb", [128, DC], F32))
        bias_m20 = e(nc.sbuf_tensor("bias_m20", [128, 1], F32))
        sc20col = e(nc.sbuf_tensor("sc20col", [128, 1], F32))
        epscol = e(nc.sbuf_tensor("epscol", [1, 1], F32))

        pb = [e(nc.psum_tensor(f"pb{n}", [128, T], F32)) for n in range(8)]
        B_Q0, B_Q1, B_K0, B_K1, B_S, B_PV0, B_SW, B_PV1 = range(8)

        sems = {s: e(nc.semaphore(s)) for s in list(SEM_OF.values()) + DMA_SEMS}

        S = Sched()

        def dma(queue, out_ap, in_ap, *waits, sem):
            S.wait(queue, *waits)
            return S.emit(queue, lambda e, o=out_ap, i=in_ap: e.dma_start(out=o, in_=i),
                          inc=16, sem=sem)

        bank_free = [None] * 8
        bank_done = [None] * 8

        def mm_group(bank, parts):
            n = len(parts)
            S.wait("tensor", bank_free[bank])
            tok = None
            for j, (out_ap, lh, rh, waits) in enumerate(parts):
                S.wait("tensor", *waits)
                first, last = j == 0, j == n - 1

                def f(e, o=out_ap, a=lh, b=rh, st=first, sp=last):
                    return e.matmul(o, a, b, start=st, stop=sp, skip_group_check=True)
                tok = S.emit("tensor", f, inc=1 if last else 0)
            bank_done[bank] = tok
            return tok

        def vop(fn, *waits):
            S.wait("vector", *waits)
            return S.emit("vector", fn, inc=1)

        def sop(fn, *waits):
            S.wait("scalar", *waits)
            return S.emit("scalar", fn, inc=1)

        ring = {"next": 0, "free": [None, None, None]}

        def load_wslice(wname, sl):
            slot = ring["next"]
            ring["next"] = (slot + 1) % 3
            tok = dma("gpsimd", wring[slot][:, :], WH[wname].ap()[:, sl],
                      ring["free"][slot], sem=f"sW{slot}")
            return slot, tok

        def wtile(slot, c, j0, width=128):
            return wring[slot][:, c * OSL + j0: c * OSL + j0 + width]

        dbg_prev = [None]

        def dbg_dump(nm, sb_ap, tok):
            if DBG and nm in dbg_h:
                t = dma("sync", dbg_h[nm].ap(), sb_ap, tok, S.tensor_now(),
                        S.vector_now(), S.scalar_now(), dbg_prev[0], sem="sC")
                dbg_prev[0] = t
                S.wait("vector", t)
                S.wait("scalar", t)
                S.wait("tensor", t)
                S.wait("gpsimd", t)


        dma("sync", scale_sb[:, :], scale_h.ap(), sem="sC")
        dma("sync", biash[:, :], bias_h.ap(), sem="sC")
        dma("sync", ident_f[:, :], ident_h.ap(), sem="sC")
        dma("gpsimd", pmat_bf[:, :], pmat_h.ap(), sem="sCg")
        tok_const = ("sC", S.cnt["sC"])
        tok_scale = tok_bias = tok_ident = tok_const
        tok_pmat = ("sCg", S.cnt["sCg"])
        dma("gpsimd", maskb[:, :], maskfst_h.ap(), sem="sM")
        dma("gpsimd", maskstr[:, :], maskstr_h.ap(), sem="sM")
        tok_mstr = ("sM", S.cnt["sM"])
        mask_cur = [tok_mstr]

        tok_ones_bf = vop(lambda e: e.memset(ones_bf[:, :], 1.0))
        tok_ones_fr = vop(lambda e: e.memset(ones_fr[:, :], 1.0))
        tok_half_fr = vop(lambda e: e.memset(half_fr[:, :], 0.5))
        tok_m20 = vop(lambda e: e.memset(bias_m20[:, :], -20.0))
        tok_eps = vop(lambda e: e.memset(epscol[:, :], 1e-5))
        tok_ms = vop(lambda e: e.memset(sc20col[:, :], S20_HI))
        tok_biash = vop(lambda e: e.tensor_single_scalar(biash[:, :], biash[:, :],
                                                         0.5, ALU.mult), tok_bias)
        tok_s20c = vop(lambda e: e.tensor_single_scalar(sc20col[:, :], sc20col[:, :],
                                                        S20_LO, ALU.add), tok_ms)

        tmp_free = [None, None]
        tmp2a_free = [None, None]
        tmp2b_free = [None, None]

        row_gate = [None]
        xsq_free = [None, None]

        def emit_ln_sums(src_bf, c, bx, bx2, first, last, waits=()):
            sq = xsq[c % 2]
            tok_sq = sop(lambda e, s=sq, i=src_bf[:, c * T:(c + 1) * T]:
                         e.activation(s[:, :], i, AF.Square),
                         *(list(waits) + [xsq_free[c % 2]]))
            S.wait("tensor", *waits)
            if first:
                S.wait("tensor", bank_free[bx], bank_free[bx2])

            S.wait("tensor", tok_ones_bf)

            def f1(e, i=src_bf[:, c * T:(c + 1) * T], st=first, sp=last, b=bx):
                return e.matmul(pb[b][0:1, :], ones_bf[:, :], i, start=st, stop=sp,
                                skip_group_check=True)
            t1_ = S.emit("tensor", f1, inc=1)
            S.wait("tensor", tok_sq)

            def f2(e, s=sq, st=first, sp=last, b=bx2):
                return e.matmul(pb[b][0:1, :], ones_bf[:, :], s[:, :], start=st, stop=sp,
                                skip_group_check=True)
            t2_ = S.emit("tensor", f2, inc=1)
            xsq_free[c % 2] = t2_
            if last:
                bank_done[bx] = ("sT", S.cnt["sT"])
                bank_done[bx2] = ("sT", S.cnt["sT"])

        mu_gate = {"a": None, "b": None, "att": None}

        def emit_ln_close(bx, bx2, mu_dst, mu_key, rr_dst, *, half, repl_bank):
            M0, V0, R0 = rowf[0:1, 0:T], rowf[0:1, T:2 * T], rowf[0:1, 0:T]
            tsum, tsq = bank_done[bx], bank_done[bx2]
            tok_mu = vop(lambda e: e.tensor_single_scalar(M0, pb[bx][0:1, :],
                                                          1.0 / DM, ALU.mult),
                         tsum, row_gate[0])
            mu_g = mm_group(repl_bank, [(pb[repl_bank][:, :], ones_fr[0:1, :],
                                         M0, [tok_mu, tok_ones_fr])])
            tok_murep = vop(lambda e, d=mu_dst: e.tensor_copy(d[:, :], pb[repl_bank][:, :]),
                            mu_g, mu_gate[mu_key])
            bank_free[repl_bank] = tok_murep
            tok_m2 = vop(lambda e: e.tensor_mul(M0, M0, M0), tok_mu, mu_g)
            tok_var = vop(lambda e: e.scalar_tensor_tensor(V0, pb[bx2][0:1, :], 1.0 / DM,
                                                           M0, ALU.mult, ALU.subtract),
                          tsq, tok_m2, row_gate[0])
            bank_free[bx] = tok_mu
            bank_free[bx2] = tok_var
            tok_ln = sop(lambda e: e.activation(V0, V0, AF.Ln, bias=epscol[:, :]),
                         tok_var, tok_eps)
            tok_r = sop(lambda e: e.activation(R0, V0, AF.Exp, scale=-0.5), tok_ln)
            lrow = half_fr if half else ones_fr
            ltok = tok_half_fr if half else tok_ones_fr
            tok_repl = mm_group(repl_bank, [(pb[repl_bank][:, :], lrow[0:1, :],
                                             R0, [tok_r, ltok])])
            tok_rr = vop(lambda e, d=rr_dst: e.tensor_copy(d[:, :], pb[repl_bank][:, :]),
                         tok_repl)
            bank_free[repl_bank] = tok_rr
            row_gate[0] = tok_repl
            return tok_murep, tok_rr

        def emit_center(dst_bf, src_f, mu_buf, mu_key, mu_tok, rr_tok):
            toks = []
            for c in range(DC):
                toks.append(vop(lambda e, cc=c, d=dst_bf, s=src_f, m=mu_buf:
                                e.tensor_tensor(d[:, cc * T:(cc + 1) * T],
                                                s[:, cc * T:(cc + 1) * T],
                                                m[:, :], ALU.subtract),
                                mu_tok, rr_tok))
            mu_gate[mu_key] = toks[-1]
            return toks

        stage_free = [None, None]
        xh_consumed = [None, None]
        xh_ready = [None, None]
        x_ready = [None] * DC
        x2read = [None] * DC
        x1_done = [None] * DC

        def emit_ffn_pair(prev_spec, next_spec, first_weights):
            deferred = []

            def do_pass3(spec, c, tmp2t, tok_p2):
                mode = spec["mode"]
                if mode == "stage_out":
                    st = stage[c % 2]
                    tok3 = vop(lambda e, s=st, t2=tmp2t, cc=c:
                               e.tensor_add(s[:, :], t2[:, :], x_f[:, cc * T:(cc + 1) * T]),
                               tok_p2, stage_free[c % 2], x_ready[c])
                    x2read[c] = tok3
                    od = dma("sync", out_h.ap()[c * 128:(c + 1) * 128,
                                                spec["tile"] * T:(spec["tile"] + 1) * T],
                             st[:, :], tok3, sem=f"sO{c % 2}")
                    stage_free[c % 2] = od
                    pc = spec.get("post_chunk")
                    if pc is not None:
                        pc(c)
                    return tok3
                elif mode == "x_inplace":
                    tok3 = vop(lambda e, t2=tmp2t, cc=c:
                               e.tensor_add(x_f[:, cc * T:(cc + 1) * T], t2[:, :],
                                            x_f[:, cc * T:(cc + 1) * T]),
                               tok_p2, x_ready[c])
                    x_ready[c] = tok3
                    x1_done[c] = tok3
                    return tok3
                else:
                    st = stage[c % 2]
                    tok3 = vop(lambda e, s=st, t2=tmp2t, cc=c:
                               e.tensor_add(s[:, :], t2[:, :], t1[cc % 2][:, :]),
                               tok_p2, stage_free[c % 2], xh_ready[c % 2])
                    xh_consumed[c % 2] = tok3
                    tokc = vop(lambda e, s=st, cc=c:
                               e.tensor_copy(att_bf[:, cc * T:(cc + 1) * T], s[:, :]), tok3)
                    stage_free[c % 2] = tokc
                    x1_done[c] = tokc
                    return tokc

            for s in range(NSL_BIG):
                slot, wtok = load_wslice("wf", s)
                last_use = None
                for member, spec in (("prev", prev_spec), ("next", next_spec)):
                    if spec is None:
                        continue
                    banks = (B_Q0, B_Q1) if member == "prev" else (B_K0, B_K1)
                    src = spec["src_bf"]
                    for jj in range(2):
                        c = s * 2 + jj
                        pc_pre = spec.get("pre_chunk")
                        if pc_pre is not None:
                            pc_pre(c)
                        bank = banks[c % 2]
                        parts = []
                        for cc in range(DC):
                            parts.append((pb[bank][:, :], wtile(slot, cc, jj * 128),
                                          src[:, cc * T:(cc + 1) * T],
                                          [wtok, spec["src_toks"][cc]]))
                        g = mm_group(bank, parts)
                        last_use = g
                        p1t = tmp[c % 2]
                        tok1 = vop(lambda e, b=bank, t=p1t, r=spec["rr"]:
                                   e.tensor_mul(t[:, :], pb[b][:, :], r[:, :]),
                                   g, spec["rr_tok"], tmp_free[c % 2])
                        bank_free[bank] = tok1
                        if member == "prev":
                            t2t, tfree = tmp[2 + c % 2], tmp2a_free
                        else:
                            t2t, tfree = tmp[4 + c % 2], tmp2b_free
                            if len(deferred) >= 2:
                                cc0, tt, tk = deferred.pop(0)
                                tmp2b_free[cc0 % 2] = do_pass3(next_spec, cc0, tt, tk)
                        tok2 = sop(lambda e, t=p1t, t2=t2t, cc=c:
                                   e.activation(t2[:, :], t[:, :], AF.Identity,
                                                bias=biash[:, cc:cc + 1],
                                                scale=scale_sb[:, cc:cc + 1]),
                                   tok1, tok_biash, tok_scale, tfree[c % 2])
                        tmp_free[c % 2] = tok2
                        if member == "prev":
                            tmp2a_free[c % 2] = do_pass3(spec, c, t2t, tok2)
                        else:
                            deferred.append((c, t2t, tok2))
                ring["free"][slot] = last_use
            for cc0, tt, tk in deferred:
                tmp2b_free[cc0 % 2] = do_pass3(next_spec, cc0, tt, tk)

        qpl_free = [None]

        def emit_rope(bank, gtok, dst_bf, dst_gate, cos_tok, par=0):
            par = 0
            qp = qpl[par]
            tokc = vop(lambda e, b=bank, q=qp: e.tensor_copy(q[:, :], pb[b][:, :]),
                       gtok, qpl_free[par])
            sw = mm_group(B_SW, [(pb[B_SW][:, :], pmat_bf[:, :], qp[:, :],
                                  [tokc, tok_pmat])])
            u1, u2 = tmp[0], tmp[1]
            tok_u1 = vop(lambda e, b=bank: e.tensor_mul(u1[:, :], pb[b][:, :], C2p[:, :]),
                         gtok, cos_tok[0], tmp_free[0])
            tok_u2 = vop(lambda e: e.tensor_mul(u2[:, :], pb[B_SW][:, :], S2p[:, :]),
                         sw, cos_tok[1], tmp_free[1])
            tok_d = vop(lambda e, d=dst_bf: e.tensor_add(d, u1[:, :], u2[:, :]),
                        tok_u1, tok_u2, dst_gate)
            tmp_free[0] = tok_d
            tmp_free[1] = tok_d
            bank_free[bank] = tok_u1
            bank_free[B_SW] = tok_u2
            qpl_free[par] = sw
            return tok_d

        kv_dve = {"k": [None] * HK, "v": None}

        def emit_kv(tile, src_bf, src_toks, rr_tok, cos_tok, r2T_tok,
                    first_weights, write_gate):
            half = tile % 2 if tile >= 0 else 1
            par = half
            for sl in range(NSL_KV):
                slot, wtok = load_wslice("wk", sl)
                last_g = None
                for gg in range(2):
                    g = sl * 2 + gg
                    bank = (B_Q0, B_Q1)[g % 2]
                    parts = []
                    for cc in range(DC):
                        parts.append((pb[bank][:, :], wtile(slot, cc, gg * 128),
                                      src_bf[:, cc * T:(cc + 1) * T],
                                      [wtok, src_toks[cc]]))
                    gt = mm_group(bank, parts)
                    last_g = gt
                    kv_dve["k"][g] = emit_rope(bank, gt, kbf[g][:, half * T:(half + 1) * T],
                                               write_gate, cos_tok)
                ring["free"][slot] = last_g
            for sl in range(NSL_KV):
                slot, wtok = load_wslice("wv", sl)
                last_g = None
                for tc in range(4):
                    bank = (B_K0, B_K1)[tc % 2]
                    parts = []
                    for cc in range(DC):
                        parts.append((pb[bank][:, 0:OSL],
                                      src_bf[:, cc * T + tc * 128: cc * T + (tc + 1) * 128],
                                      wring[slot][:, cc * OSL:(cc + 1) * OSL],
                                      [wtok, src_toks[cc]]))
                    gt = mm_group(bank, parts)
                    last_g = gt
                    dst = vbfT[par][:, tc * T + sl * OSL: tc * T + (sl + 1) * OSL]
                    tokv = vop(lambda e, d=dst, b=bank, tcc=tc:
                               e.tensor_scalar(d, pb[b][:, 0:OSL], r2T[:, tcc:tcc + 1],
                                               None, ALU.mult),
                               gt, r2T_tok, write_gate)
                    bank_free[bank] = tokv
                    kv_dve["v"] = tokv
                ring["free"][slot] = last_g

        xn_free = [None]
        attn = {"e_free": [None, None], "attnout": [None] * H, "attnout_free": [None] * H}
        t1_free = [None, None, None]
        recipS_free = [None, None]
        kv_gate = [None]

        def emit_attention(tile, src_bf, src_toks, rr_tok, cos_tok,
                           mask_tok, kv_all, first_weights):
            par = tile % 2
            kprev, kcur = 1 - par, par
            q_rope = [None] * (H + 1)
            qslot = [None]
            qbf_gate = [None, None]

            def group_closures(bank, parts, out):
                n = len(parts)
                cls = []
                for j, (out_ap, lh, rh, waits) in enumerate(parts):
                    def emit_j(j=j, out_ap=out_ap, lh=lh, rh=rh, waits=waits):
                        if j == 0:
                            S.wait("tensor", bank_free[bank])
                        S.wait("tensor", *waits)

                        def f(e, o=out_ap, a=lh, b=rh, st=(j == 0), sp=(j == n - 1)):
                            return e.matmul(o, a, b, start=st, stop=sp,
                                            skip_group_check=True)
                        tok = S.emit("tensor", f, inc=1 if j == n - 1 else 0)
                        if j == n - 1:
                            bank_done[bank] = tok
                            out[0] = tok
                    cls.append(emit_j)
                return cls

            def q_parts(h):
                if h >= H:
                    return None
                if h % 2 == 0:
                    qslot[0] = load_wslice("wq", h // 2)
                slot, wtok = qslot[0]
                bank = (B_Q0, B_Q1)[h % 2]
                parts = []
                for cc in range(DC):
                    parts.append((pb[bank][:, :], wtile(slot, cc, (h % 2) * 128),
                                  src_bf[:, cc * T:(cc + 1) * T],
                                  [wtok, src_toks[cc]]))
                out = [None]
                return (h, bank, slot, out, group_closures(bank, parts, out))

            def q_finish(spec):
                if spec is None:
                    return
                h, bank, slot, out, _ = spec
                if h % 2 == 1:
                    ring["free"][slot] = out[0]
                q_rope[h] = emit_rope(bank, out[0], qbf[h % 2][:, :],
                                      qbf_gate[h % 2], cos_tok, par=h % 2)

            pv_state = {}

            def pv_a_parts(h):
                g = h // 4
                ep = e_bf[h % 2]
                pvb = (B_PV0, B_PV1)[h % 2]
                parts_pv, parts_s = [], []
                for kc in (3, 4, 0, 1, 2, 5, 6, 7):
                    khalf = kprev if kc < 4 else kcur
                    tc = kc % 4
                    tok_e, q0, q1 = e_toks_h[h][kc]
                    vt = vbfT[khalf][:, tc * T + g * 128: tc * T + (g + 1) * 128]
                    parts_pv.append((pb[pvb][:, q0:q1], vt,
                                     ep[:, kc * T + q0:kc * T + q1],
                                     [tok_e, kv_all, kv_dve["v"]]))
                    parts_s.append((pb[B_S][0:1, q0:q1], ones_bf[:, :],
                                    ep[:, kc * T + q0:kc * T + q1],
                                    [tok_e, tok_ones_bf]))
                out_pv, out_s = [None], [None]
                cls = group_closures(pvb, parts_pv, out_pv) +                     group_closures(B_S, parts_s, out_s)
                return (h, out_pv, out_s, cls)

            def pv_a_finish(spec):
                h, out_pv, out_s, _ = spec
                attn["e_free"][h % 2] = ("sT", S.cnt["sT"])
                tok_srow = sop(lambda e: e.copy(rowf[0:1, 0:T], pb[B_S][0:1, :]),
                               out_s[0], row_gate[0])
                bank_free[B_S] = tok_srow
                pv_state[h] = (out_pv[0], tok_srow)

            def pv_head_b(h):
                pv_g, tok_srow = pv_state.pop(h)
                pvb = (B_PV0, B_PV1)[h % 2]
                rep_g = mm_group(B_S, [(pb[B_S][:, :], ones_fr[0:1, :],
                                        rowf[0:1, 0:T], [tok_srow, tok_ones_fr])])
                row_gate[0] = rep_g
                tok_rec = vop(lambda e, p=h % 2: e.reciprocal(recipS[p][:, :],
                                                              pb[B_S][:, :]),
                              rep_g, recipS_free[h % 2])
                bank_free[B_S] = tok_rec
                tok_ao = vop(lambda e, b=pvb, p=h % 2, hh=h:
                             e.tensor_mul(att_bf[:, hh * T:(hh + 1) * T], pb[b][:, :],
                                          recipS[p][:, :]),
                             pv_g, tok_rec, attn["attnout_free"][h])
                recipS_free[h % 2] = tok_ao
                bank_free[pvb] = tok_ao
                attn["attnout"][h] = tok_ao

            e_toks_h = [None] * H
            fast = (tile > 0)
            qs0 = q_parts(0)
            for c in qs0[4]:
                c()
            q_finish(qs0)
            for h in range(H):
                g = h // 4
                qs = q_parts(h + 1)
                fill = list(qs[4]) if qs is not None else []
                pa = None
                if h > 0:
                    pa = pv_a_parts(h - 1)
                    fill += pa[3]
                e_toks = []
                for kc in range(8):
                    bank = (B_K0, B_K1)[kc % 2]
                    tsl = kc % 3
                    khalf = kprev if kc < 4 else kcur
                    if fast:
                        if kc < 4:
                            q0, q1 = 0, 128 * (kc + 1)
                            s0, moff = 128 * kc, 0
                        else:
                            q0, q1 = 128 * (kc - 4), T
                            s0, moff = 128 * (kc - 4), 128
                    else:
                        q0, q1, s0, moff = 0, T, None, None
                    lh = kbf[g][:, khalf * T + (kc % 4) * 128: khalf * T + (kc % 4) * 128 + 128]
                    sc_g = mm_group(bank, [(pb[bank][:, q0:q1], lh, qbf[h % 2][:, q0:q1],
                                            [q_rope[h], kv_all, kv_dve["k"][g]])])
                    tt = t1[tsl]
                    tok_t = sop(lambda e, b=bank, t=tt, a=q0, z=q1:
                                e.activation(t[:, a:z], pb[b][:, a:z], AF.Tanh,
                                             scale=sc20col[:, :]),
                                sc_g, tok_s20c, t1_free[tsl])
                    bank_free[bank] = tok_t
                    if fast:
                        tok_m = vop(lambda e, t=tt, a=s0, mo=moff:
                                    e.tensor_add(t[:, a:a + 128], t[:, a:a + 128],
                                                 maskstr[:, mo:mo + 128]),
                                    tok_t, tok_mstr)
                        tok_e = sop(lambda e, t=tt, kcc=kc, p=h % 2, a=q0, z=q1:
                                    e.activation(e_bf[p][:, kcc * T + a:kcc * T + z],
                                                 t[:, a:z], AF.Exp, scale=CAP,
                                                 bias=bias_m20[:, :]),
                                    tok_m, attn["e_free"][h % 2], tok_m20)
                    else:
                        tok_m = vop(lambda e, t=tt, kcc=kc:
                                    e.scalar_tensor_tensor(t[:, :], t[:, :], CAP,
                                                           maskb[:, kcc * T:(kcc + 1) * T],
                                                           ALU.mult, ALU.add),
                                    tok_t, mask_tok)
                        tok_e = sop(lambda e, t=tt, kcc=kc, p=h % 2:
                                    e.activation(e_bf[p][:, kcc * T:(kcc + 1) * T], t[:, :],
                                                 AF.Exp, bias=bias_m20[:, :]),
                                    tok_m, attn["e_free"][h % 2], tok_m20)
                    t1_free[tsl] = tok_e
                    e_toks.append((tok_e, q0, q1))
                    for _ in range(4):
                        if fill:
                            fill.pop(0)()
                for c in fill:
                    c()
                e_toks_h[h] = e_toks
                qbf_gate[h % 2] = ("sT", S.cnt["sT"])
                q_finish(qs)
                if pa is not None:
                    pv_a_finish(pa)
                if h > 0:
                    pv_head_b(h - 1)
            pa = pv_a_parts(H - 1)
            for c in pa[3]:
                c()
            pv_a_finish(pa)
            pv_head_b(H - 1)

        xnbf_tok = [None] * DC
        x2bf_tok = [None] * DC
        x1bf_tok = [None] * DC

        def emit_wo(tile, have_next, xsbf_gate):
            woslot = [None]
            for oc in range(DC):
                if oc % 2 == 0:
                    woslot[0] = load_wslice("wo", oc // 2)
                slot, wtok = woslot[0]
                bank = (B_Q0, B_Q1)[oc % 2]
                parts = []
                for cc in range(DC):
                    parts.append((pb[bank][:, :], wtile(slot, cc, (oc % 2) * 128),
                                  att_bf[:, cc * T:(cc + 1) * T],
                                  [wtok, attn["attnout"][cc]]))
                g = mm_group(bank, parts)
                if oc % 2 == 1:
                    ring["free"][slot] = g
                tok2 = vop(lambda e, b=bank, c=oc:
                           e.tensor_add(x_f[:, c * T:(c + 1) * T], pb[b][:, :],
                                        x_f[:, c * T:(c + 1) * T]),
                           g, x_ready[oc])
                bank_free[bank] = tok2
                x_ready[oc] = tok2
                tokc = vop(lambda e, c=oc:
                           e.tensor_copy(xs_bf[:, c * T:(c + 1) * T],
                                         x_f[:, c * T:(c + 1) * T]),
                           tok2, xsbf_gate)
                x2bf_tok[oc] = tokc
                if oc > 0:
                    emit_ln_sums(xs_bf, oc - 1, B_S, B_PV0, oc - 1 == 0, False,
                                 waits=(x2bf_tok[oc - 1],))
                    if have_next:
                        emit_ln_sums(xn_bf, oc - 1, B_K0, B_K1, oc - 1 == 0, False,
                                     waits=(xnbf_tok[oc - 1],))
            emit_ln_sums(xs_bf, DC - 1, B_S, B_PV0, False, True,
                         waits=(x2bf_tok[DC - 1],))
            if have_next:
                emit_ln_sums(xn_bf, DC - 1, B_K0, B_K1, False, True,
                             waits=(xnbf_tok[DC - 1],))

        def emit_ln2_close(cos_toks):
            mu2, rr2 = emit_ln_close(B_S, B_PV0, mu_att, "att", rr_att, half=False,
                                     repl_bank=B_SW)
            tc1 = vop(lambda e: e.tensor_mul(C2p[:, :], c2t[:, :], rr_att[:, :]),
                      rr2, cos_toks[0])
            tc2 = vop(lambda e: e.tensor_mul(S2p[:, :], s2t[:, :], rr_att[:, :]),
                      rr2, cos_toks[1])
            last_t = None
            for tc in range(4):
                S.wait("tensor", bank_free[B_S], rr2, tok_ident)

                def ftr(e, tcc=tc):
                    return e.transpose(pb[B_S][:, 0:128],
                                       rr_att[:, tcc * 128:(tcc + 1) * 128], ident_f[:, :])
                gtr = S.emit("tensor", ftr, inc=1)
                last_t = vop(lambda e, tcc=tc:
                             e.tensor_copy(r2T[:, tcc:tcc + 1], pb[B_S][:, 0:1]), gtr)
                bank_free[B_S] = last_t
            return mu2, rr2, (tc1, tc2), last_t

        for c in range(DC):
            x_ready[c] = dma("sync", x_f[:, c * T:(c + 1) * T],
                             x_h.ap()[c * 128:(c + 1) * 128, T:2 * T], sem=f"sX{c}")
        for c in range(DC):
            dma("gpsimd", xs_bf[:, c * T:(c + 1) * T],
                x_h.ap()[c * 128:(c + 1) * 128, 0:T], sem="sHB")
        tok_hb = ("sHB", S.cnt["sHB"])
        h_xbf_tok = [tok_hb] * DC
        for c in range(DC):
            xnbf_tok[c] = dma("gpsimd", xn_bf[:, c * T:(c + 1) * T],
                              x_h.ap()[c * 128:(c + 1) * 128, T:2 * T], sem=f"sN{c}")

        for c in range(DC):
            emit_ln_sums(xs_bf, c, B_S, B_PV0, c == 0, c == DC - 1,
                         waits=(h_xbf_tok[c],))
        mu_h, rr_h = emit_ln_close(B_S, B_PV0, mu_a, "a", rr_a, half=True,
                                   repl_bank=B_SW)
        cen_h = emit_center(xs_bf, xs_bf, mu_a, "a", mu_h, rr_h)
        for c in range(DC):
            emit_ln_sums(xn_bf, c, B_S, B_PV0, c == 0, c == DC - 1,
                         waits=(xnbf_tok[c],))
        mu_0, rr_0 = emit_ln_close(B_S, B_PV0, mu_b, "b", rr_b, half=True,
                                   repl_bank=B_SW)
        cen_0 = emit_center(xn_bf, xn_bf, mu_b, "b", mu_0, rr_0)

        def halo_pre_chunk(c):
            xh_ready[c % 2] = dma("sync", t1[c % 2][:, :],
                                  x_h.ap()[c * 128:(c + 1) * 128, 0:T],
                                  xh_consumed[c % 2], sem=f"sH{c % 2}")

        halo_spec = dict(src_bf=xs_bf, src_toks=cen_h, rr=rr_a, rr_tok=rr_h,
                         mode="halo", tile=-1, pre_chunk=halo_pre_chunk)
        t0_spec = dict(src_bf=xn_bf, src_toks=cen_0, rr=rr_b, rr_tok=rr_0,
                       mode="x_inplace", tile=0)
        if PAIR:
            emit_ffn_pair(halo_spec, t0_spec, first_weights=True)
        else:
            emit_ffn_pair(halo_spec, None, first_weights=True)
            emit_ffn_pair(None, t0_spec, first_weights=False)
        pair_done = S.tensor_now()

        for c in range(DC):
            x1bf_tok[c] = vop(lambda e, cc=c:
                              e.tensor_copy(xs_bf[:, cc * T:(cc + 1) * T],
                                            x_f[:, cc * T:(cc + 1) * T]),
                              x1_done[c], pair_done)

        dbg_dump("d_x1bf", xs_bf[:, :], x1bf_tok[DC - 1])
        for c in range(DC):
            emit_ln_sums(att_bf, c, B_S, B_PV0, c == 0, c == DC - 1,
                         waits=(x1_done[c],))
        dma("sync", c2t[:, :], cos2_h.ap()[:, 0:T], sem="sCT")
        st_h = dma("sync", s2t[:, :], sin2_h.ap()[:, 0:T], sem="sCT")
        mu2_h, rr2_h, cs_h, r2T_h = emit_ln2_close((st_h, st_h))
        cs_free = cs_h[1]
        cen2h = emit_center(att_bf, att_bf, mu_att, "att", mu2_h, rr2_h)
        emit_kv(-1, att_bf, cen2h, rr2_h, cs_h, r2T_h,
                first_weights=True, write_gate=None)
        kv_all = S.tensor_now()
        for hh in range(H):
            attn["attnout_free"][hh] = kv_all
        kv_gate[0] = None

        for i in range(NT):
            have_next = i < NT - 1
            dma("sync", c2t[:, :], cos2_h.ap()[:, (i + 1) * T:(i + 2) * T], cs_free, sem="sCT")
            st_i = dma("sync", s2t[:, :], sin2_h.ap()[:, (i + 1) * T:(i + 2) * T], cs_free, sem="sCT")
            ct_i = st_i
            for c in range(DC):
                emit_ln_sums(xs_bf, c, B_S, B_PV0, c == 0, c == DC - 1,
                             waits=(x1bf_tok[c],))
            mu2, rr2, cs_toks, r2T_tok = emit_ln2_close((ct_i, st_i))
            cs_free = cs_toks[1]
            cen2 = emit_center(xs_bf, x_f, mu_att, "att", mu2, rr2)
            for c in range(DC):
                x_ready[c] = cen2[c]

            if have_next:
                xn_gate = xn_free[0] if i > 0 else pair_done
                for c in range(DC):
                    xnbf_tok[c] = dma("gpsimd", xn_bf[:, c * T:(c + 1) * T],
                                      x_h.ap()[c * 128:(c + 1) * 128,
                                               (i + 2) * T:(i + 3) * T], xn_gate,
                                      sem=f"sN{c}")

            src_toks = cen2
            emit_kv(i, xs_bf, src_toks, rr2, cs_toks, r2T_tok,
                    first_weights=False, write_gate=kv_gate[0])
            kv_all = S.tensor_now()

            if i == 0:
                dbg_dump("d_kbf0", kbf[0][:, :], kv_all)
                dbg_dump("d_vbfT", vbfT[0][:, :], kv_all)
                dbg_dump("d_C2p", C2p[:, :], cs_toks[1])
                dbg_dump("d_S2p", S2p[:, :], cs_toks[1])
                dbg_dump("d_rr", rr_att[:, :], rr2)
                dbg_dump("d_r2T", r2T[:, 0:4], r2T_tok)
                dbg_dump("d_cos", cos2_h.ap(), None)
                dbg_dump("d_sin", sin2_h.ap(), None)
            emit_attention(i, xs_bf, src_toks, rr2, cs_toks, mask_cur[0],
                           kv_all, first_weights=(i == 0))
            if i == 0:
                dbg_dump("d_att", att_bf[:, :], attn["attnout"][H - 1])
                dbg_dump("d_qbf0", qbf[0][:, :], attn["attnout"][H - 1])
                dbg_dump("d_e0", e_bf[1][:, :], attn["attnout"][H - 1])
            kv_gate[0] = S.tensor_now()
            attn_done = S.tensor_now()


            emit_wo(i, have_next, xsbf_gate=attn_done)
            wo_done = S.tensor_now()
            if i == 0:
                dbg_dump("d_x2bf", xs_bf[:, :], x2bf_tok[DC - 1])
            for hh in range(H):
                attn["attnout_free"][hh] = wo_done

            mu3, rr3 = emit_ln_close(B_S, B_PV0, mu_a, "a", rr_a, half=True,
                                     repl_bank=B_SW)
            cen3 = emit_center(xs_bf, x_f, mu_a, "a", mu3, rr3)
            if have_next:
                mu1n, rr1n = emit_ln_close(B_K0, B_K1, mu_b, "b", rr_b, half=True,
                                           repl_bank=B_SW)
                cen1n = emit_center(xn_bf, xn_bf, mu_b, "b", mu1n, rr1n)

            def mk_post_chunk(ii):
                def post_chunk(c):
                    x_ready[c] = dma("sync", x_f[:, c * T:(c + 1) * T],
                                     x_h.ap()[c * 128:(c + 1) * 128,
                                              (ii + 2) * T:(ii + 3) * T], x2read[c],
                                     sem=f"sX{c}")
                return post_chunk

            prev_spec = dict(src_bf=xs_bf, src_toks=cen3, rr=rr_a, rr_tok=rr3,
                             mode="stage_out", tile=i,
                             post_chunk=mk_post_chunk(i) if have_next else None)
            next_spec = None
            if have_next:
                next_spec = dict(src_bf=xn_bf, src_toks=cen1n, rr=rr_b, rr_tok=rr1n,
                                 mode="x_inplace", tile=i + 1)
            if PAIR:
                emit_ffn_pair(prev_spec, next_spec, first_weights=False)
            else:
                emit_ffn_pair(prev_spec, None, first_weights=False)
                if next_spec is not None:
                    emit_ffn_pair(None, next_spec, first_weights=False)
            pair_done = S.tensor_now()
            xn_free[0] = pair_done
            if have_next:
                for c in range(DC):
                    x1bf_tok[c] = vop(lambda e, cc=c:
                                      e.tensor_copy(xs_bf[:, cc * T:(cc + 1) * T],
                                                    x_f[:, cc * T:(cc + 1) * T]),
                                      x1_done[c], pair_done)

        S.ops["sync"].append((("W", "sO0", S.cnt["sO0"]), None, 0))
        S.ops["sync"].append((("W", "sO1", S.cnt["sO1"]), None, 0))

        def realize_one(name):
            def f(engine):
                for fn, sem, inc in S.ops[name]:
                    if isinstance(fn, tuple) and fn[0] == "W":
                        engine.wait_ge(sems[fn[1]], fn[2])
                    else:
                        inst = fn(engine)
                        if sem is not None:
                            inst.then_inc(sems[sem], inc)
            return f

        block = e(nc.Block())
        block.sync(realize_one("sync"))
        block.gpsimd(realize_one("gpsimd"))
        block.tensor(realize_one("tensor"))
        block.vector(realize_one("vector"))
        block.scalar(realize_one("scalar"))

    return nc


def _layout_weight(w):
    O = w.shape[0]
    nsl = O // OSL
    wt = np.ascontiguousarray(w.T)
    L = wt.reshape(DC, 128, nsl, OSL).transpose(1, 2, 0, 3)
    return np.ascontiguousarray(L, dtype=np.float32)


def _build_masks(h):
    kk = np.arange(8 * 128)[:, None]
    a = np.arange(T)[None, :]
    allowed = (a <= kk) & (kk <= a + T)
    m = np.where(allowed, 0.0, NEG_MASK).astype(np.float32)
    m_std = np.ascontiguousarray(
        m.reshape(8, 128, T).transpose(1, 0, 2).reshape(128, 8 * T))
    if h == 0:
        mf2 = m.reshape(8, 128, T).copy()
        mf2[0:4] = NEG_MASK
        m_first = np.ascontiguousarray(mf2.transpose(1, 0, 2).reshape(128, 8 * T))
    else:
        m_first = m_std
    p = np.arange(128)[:, None]
    ap = np.arange(128)[None, :]
    strip_prev = np.where(ap <= p, 0.0, NEG_MASK / CAP).astype(np.float32)
    strip_cur = np.where(ap >= p, 0.0, NEG_MASK / CAP).astype(np.float32)
    m_strips = np.ascontiguousarray(np.concatenate([strip_prev, strip_cur], axis=1))
    return m_strips, m_first


def _rope_tables_f32():
    import jax
    import jax.numpy as jnp
    cpu = jax.devices("cpu")[0]
    with jax.default_device(cpu):
        inv = 1.0 / 10000.0 ** (jnp.arange(0, 128, 2, dtype=jnp.float32) / 128)
        ang = jnp.arange(8192, dtype=jnp.float32)[:, None] * inv
        cos = np.asarray(jnp.cos(ang))
        sin = np.asarray(jnp.sin(ang))
    c2 = np.repeat(cos.T, 2, axis=0)
    s2 = np.empty((128, 8192), dtype=np.float32)
    s2[0::2] = -sin.T
    s2[1::2] = sin.T
    return np.ascontiguousarray(c2), np.ascontiguousarray(s2)


def kernel(L0_attn_x, L0_attn_wq, L0_attn_wk, L0_attn_wv, L0_attn_wo,
           L0_ffn_weight, L0_ffn_scale_weight, L0_ffn_bias):
    x = np.asarray(L0_attn_x, dtype=np.float32)
    wq = _layout_weight(np.asarray(L0_attn_wq, dtype=np.float32))
    wk = _layout_weight(np.asarray(L0_attn_wk, dtype=np.float32))
    wv = _layout_weight(np.asarray(L0_attn_wv, dtype=np.float32))
    wo = _layout_weight(np.asarray(L0_attn_wo, dtype=np.float32))
    wf = _layout_weight(np.asarray(L0_ffn_weight, dtype=np.float32))
    scale = np.ascontiguousarray(
        np.asarray(L0_ffn_scale_weight, dtype=np.float32).reshape(DC, 128).T)
    bias = np.ascontiguousarray(
        np.asarray(L0_ffn_bias, dtype=np.float32).reshape(DC, 128).T)
    pmat = np.zeros((128, 128), dtype=np.float32)
    idx = np.arange(128)
    pmat[idx, idx ^ 1] = 1.0
    ident = np.eye(128, dtype=np.float32)
    c2full, s2full = _rope_tables_f32()

    nc = build_nc()
    in_maps = []
    for cid in range(8):
        b, h = cid // 2, cid % 2
        lo = h * 4096
        xc = np.zeros((SPAN, DM), dtype=np.float32)
        if h == 0:
            xc[T:] = x[b, 0:4096]
        else:
            xc[:] = x[b, 4096 - T:8192]
        xc = np.ascontiguousarray(xc.T)
        posv = np.maximum(np.arange(lo - T, lo + 4096), 0)
        cos2 = np.ascontiguousarray(c2full[:, posv])
        sin2 = np.ascontiguousarray(s2full[:, posv])
        m_strips, m_first = _build_masks(h)
        in_maps.append({
            "x": xc, "wq": wq, "wk": wk, "wv": wv, "wo": wo, "wf": wf,
            "scale": scale, "bias": bias, "cos2_in": cos2, "sin2_in": sin2,
            "mask_strips": m_strips, "mask_first": m_first,
            "pmat": pmat, "ident": ident,
        })
    res = run_bass_kernel_spmd(nc, in_maps, core_ids=list(range(8)))
    out = np.empty((4, 8192, DM), dtype=np.float32)
    for cid in range(8):
        b, h = cid // 2, cid % 2
        out[b, h * 4096:(h + 1) * 4096, :] = res.results[cid]["out"].T
    return out
